# version 01 (final)
# baseline (speedup 1.0000x reference)
import sys
from contextlib import ExitStack

import numpy as np

sys.path.insert(0, "/opt/trn_rl_repo")
import concourse.bass as bass
import concourse.mybir as mybir
from concourse.bass_utils import run_bass_kernel_spmd

F32 = mybir.dt.float32
BF16 = mybir.dt.bfloat16
AF = mybir.ActivationFunctionType
ALU = mybir.AluOpType
AX = mybir.AxisListType

D = 2048
SEQ = 2048
SH = 1024
NE = 8
DC = 16
HC = 8
EXP_SHIFT = 32.0
STAGE = 3
EXP_LIST = list(range(9))
DEBUG_DUMP = None


def build_program(stage=STAGE):
    nc = bass.Bass("TRN2", target_bir_lowering=False)

    xfm = nc.dram_tensor("xfm", [D, SEQ], F32, kind="ExternalInput").ap()
    wq_d = nc.dram_tensor("wq", [D, D], F32, kind="ExternalInput").ap()
    wk_d = nc.dram_tensor("wk", [D, D], F32, kind="ExternalInput").ap()
    wvs = nc.dram_tensor("wvs", [16, 128, 2048], F32, kind="ExternalInput").ap()
    wos = nc.dram_tensor("wos", [16, 128, 2048], F32, kind="ExternalInput").ap()
    gw = nc.dram_tensor("gw", [8, 2048], F32, kind="ExternalInput").ap()
    w1s = nc.dram_tensor("w1s", [64, 128, 2048], F32, kind="ExternalInput").ap()
    w3s = nc.dram_tensor("w3s", [64, 128, 2048], F32, kind="ExternalInput").ap()
    w2s = nc.dram_tensor("w2s", [128, 128, 1024], F32, kind="ExternalInput").ap()
    sw1s = nc.dram_tensor("sw1s", [8, 128, 2048], F32, kind="ExternalInput").ap()
    sw3s = nc.dram_tensor("sw3s", [8, 128, 2048], F32, kind="ExternalInput").ap()
    sw2s = nc.dram_tensor("sw2s", [16, 128, 1024], F32, kind="ExternalInput").ap()
    ident = nc.dram_tensor("ident", [128, 128], F32, kind="ExternalInput").ap()
    out_d = nc.dram_tensor("out", [16, 128, SH], F32, kind="ExternalOutput").ap()

    y2t = nc.dram_tensor("y2t", [16, 128, SEQ], BF16).ap()
    q_dr = nc.dram_tensor("q_dr", [16, 128, 2048], BF16).ap()
    k_dr = nc.dram_tensor("k_dr", [16, 128, 2048], BF16).ap()
    v_sp = nc.dram_tensor("v_sp", [16, 128, SH], BF16).ap()
    r_sp = nc.dram_tensor("r_sp", [16, 128, SH], F32).ap()

    g = ExitStack()

    def gsb(name, shape, dt):
        return g.enter_context(nc.sbuf_tensor(name, shape, dt))

    class Ctr:
        def __init__(self, handle):
            self.h = handle
            self.n = 0

        def inc(self, by=1):
            self.n += by
            return self.n

    sem_names = ["w", "s", "k", "pe", "act", "dve", "me",
                 "zd", "zpf", "a1", "ss", "hs", "pod", "pof",
                 "zf1", "zf3", "wf", "gbc", "gbm",
                 "b_t", "b_v", "b_a", "b_g", "b_y"]
    sem = {nm: Ctr(g.enter_context(nc.semaphore("sm_" + nm))) for nm in sem_names}
    W, SY, KQ = sem["w"], sem["s"], sem["k"]
    PE, ACT, DVE = sem["pe"], sem["act"], sem["dve"]

    ones1 = gsb("ones1", [1, 128], F32)
    onescale = gsb("onescale", [128, 1], F32)
    onescol_bf = gsb("onescol_bf", [128, 1], BF16)
    eps11 = gsb("eps11", [1, 1], F32)
    expbias = gsb("expbias", [128, 1], F32)
    idf = gsb("idf", [128, 128], F32)
    idb = gsb("idb", [128, 128], BF16)
    gate_T = gsb("gate_T", [128, DC * 8], BF16)
    rz_sb = gsb("rz_sb", [128, 16], F32)

    bar = [0]
    bar_sems = ["b_t", "b_v", "b_a", "b_g", "b_y"]
    data_sems = sem_names[:19]

    def barrier():
        bar[0] += 1
        tgt = bar[0]
        for bn in bar_sems:
            sem[bn].n = tgt
        snap = [(sem[nm].h, sem[nm].n) for nm in data_sems]
        with nc.Block() as block:
            for engname, bn in [("tensor", "b_t"), ("vector", "b_v"),
                                ("scalar", "b_a"), ("gpsimd", "b_g"),
                                ("sync", "b_y")]:
                def mk(bn=bn):
                    def f(eng):
                        for h, n in snap:
                            eng.wait_ge(h, n)
                        eng.sem_inc(sem[bn].h, 1)
                        for bn2 in bar_sems:
                            eng.wait_ge(sem[bn2].h, tgt)
                    return f
                getattr(block, engname)(mk())

    with nc.sbuf_tensor("gwb", [8, 2048], BF16) as gwb, \
         nc.psum_tensor("pgt", [128, 8], BF16) as pgt:
        with nc.Block() as block:
            @block.gpsimd
            def _(gp):
                gp.memset(ones1[:, :], 1.0)
                gp.memset(onescale[:, :], 1.0 / D)
                gp.memset(onescol_bf[:, :], 1.0)
                gp.memset(eps11[:, :], 1e-5)
                gp.memset(expbias[:, :], -EXP_SHIFT)
                gp.dma_start(out=idf[:, :], in_=ident).then_inc(W.h, 16)
                gp.dma_start(out=idb[:, :], in_=ident).then_inc(W.h, 16)
                gp.dma_start(out=gwb[:, :], in_=gw).then_inc(W.h, 16)
                W.inc(48)

            ld_t, pe0, act0 = W.n, PE.n, ACT.n

            @block.tensor
            def _(te):
                te.wait_ge(W.h, ld_t)
                for dc in range(DC):
                    if dc >= 1:
                        te.wait_ge(ACT.h, act0 + dc)
                    te.transpose(pgt[:, :], gwb[0:8, dc * 128:(dc + 1) * 128],
                                 idb[0:8, 0:8]).then_inc(PE.h, 1)
                    PE.inc()

            @block.scalar
            def _(sc):
                for dc in range(DC):
                    sc.wait_ge(PE.h, pe0 + dc + 1)
                    sc.activation(gate_T[:, dc * 8:(dc + 1) * 8], pgt[:, :],
                                  AF.Copy).then_inc(ACT.h, 1)
                    ACT.inc()
        barrier()

    def emit_ffn_pass(tok0, produce_y2, spill_r, final_out, R, load_x,
                      qscale=1.0):
        stk = ExitStack()
        tag = f"_f{tok0}_{int(produce_y2)}{int(final_out)}"

        def sb(n, s_, d):
            return stk.enter_context(nc.sbuf_tensor(n + tag, s_, d))

        def psm(n, s_, d):
            return stk.enter_context(nc.psum_tensor(n + tag, s_, d))

        yb = sb("yb", [128, DC * 1024], BF16)
        scr = [sb(f"scr{i}", [128, 1024], F32) for i in range(3)]
        he = sb("he", [128, HC * 1024], BF16)
        w1b = [sb(f"w1b{i}", [128, 4096], BF16) for i in range(2)]
        w3b = [sb(f"w3b{i}", [128, 4096], BF16) for i in range(2)]
        w2b = [sb(f"w2b{i}", [128, 2048], BF16) for i in range(2)]
        gb_sb = [sb(f"gb{i}", [128, 1024], F32) for i in range(2)]
        y2st = [sb(f"y2st{i}", [128, 1024], BF16) for i in range(2)]
        mrow = sb("mrow", [1, 1024], F32)
        vrow = sb("vrow", [1, 1024], F32)
        trow = sb("trow", [1, 1024], F32)
        g_tok = sb("g_tok", [128, 64], F32)
        gm1 = sb("gm1", [128, 1], F32)
        gm2 = sb("gm2", [128, 1], F32)
        gtmp = sb("gtmp", [128, 8], F32)
        g_em = sb("g_em", [8, 1024], F32)
        g_row = sb("g_row", [1, 1024], F32)
        P0 = psm("P0", [128, 1024], F32)
        P1 = psm("P1", [128, 1024], F32)
        P2 = psm("P2", [128, 1024], F32)
        P3 = psm("P3", [128, 1024], F32)

        def emit_ln(to_y2t):
            with nc.Block() as block:
                xld = {}
                if load_x and not to_y2t:
                    @block.sync
                    def _(sy):
                        for dc in range(DC):
                            sy.dma_start(
                                out=R[:, dc * 1024:(dc + 1) * 1024],
                                in_=xfm[dc * 128:(dc + 1) * 128, tok0:tok0 + 1024]
                            ).then_inc(SY.h, 16)
                            SY.inc(16)
                            xld[dc] = SY.n

                pe0, act0 = PE.n, ACT.n

                @block.scalar
                def _(sc):
                    for dc in range(DC):
                        if dc in xld:
                            sc.wait_ge(SY.h, xld[dc])
                        if dc >= 2:
                            sc.wait_ge(PE.h, pe0 + dc - 1)
                        sc.activation(scr[dc % 2][:, :],
                                      R[:, dc * 1024:(dc + 1) * 1024],
                                      AF.Square).then_inc(ACT.h, 1)
                        ACT.inc()

                @block.tensor
                def _(te):
                    for dc in range(DC):
                        if dc in xld:
                            te.wait_ge(SY.h, xld[dc])
                        st, sp = dc == 0, dc == DC - 1
                        te.matmul(P0[0:1, 0:512], onescale[:, 0:1],
                                  R[:, dc * 1024:dc * 1024 + 512],
                                  start=st, stop=sp, skip_group_check=True)
                        te.matmul(P0[0:1, 512:1024], onescale[:, 0:1],
                                  R[:, dc * 1024 + 512:(dc + 1) * 1024],
                                  start=st, stop=sp, skip_group_check=True)
                        te.wait_ge(ACT.h, act0 + dc + 1)
                        te.matmul(P1[0:1, 0:512], onescale[:, 0:1],
                                  scr[dc % 2][:, 0:512],
                                  start=st, stop=sp, skip_group_check=True)
                        te.matmul(P1[0:1, 512:1024], onescale[:, 0:1],
                                  scr[dc % 2][:, 512:1024], start=st, stop=sp,
                                  skip_group_check=True).then_inc(PE.h, 1)
                        PE.inc()

                st_pe = PE.n

                @block.scalar
                def _(sc):
                    sc.wait_ge(PE.h, st_pe)
                    sc.activation(mrow[:, :], P0[0:1, :], AF.Copy)
                    sc.activation(vrow[:, :], P1[0:1, :],
                                  AF.Copy).then_inc(ACT.h, 1)
                    ACT.inc()
                a1_ = ACT.n

                @block.vector
                def _(ve):
                    ve.wait_ge(ACT.h, a1_)
                    ve.tensor_tensor(trow[:, :], mrow[:, :], mrow[:, :],
                                     op=ALU.mult)
                    ve.tensor_tensor(vrow[:, :], vrow[:, :], trow[:, :],
                                     op=ALU.subtract).then_inc(DVE.h, 1)
                    DVE.inc()
                d1_ = DVE.n

                @block.scalar
                def _(sc):
                    sc.wait_ge(DVE.h, d1_)
                    sc.activation(vrow[:, :], vrow[:, :], AF.Sqrt,
                                  bias=eps11[0:1, 0:1]).then_inc(ACT.h, 1)
                    ACT.inc()
                a2_ = ACT.n

                @block.vector
                def _(ve):
                    ve.wait_ge(ACT.h, a2_)
                    ve.reciprocal(trow[:, :], vrow[:, :]).then_inc(DVE.h, 1)
                    DVE.inc()
                d2_ = DVE.n

                @block.tensor
                def _(te):
                    te.wait_ge(DVE.h, d2_)
                    te.matmul(P2[:, 0:512], ones1[0:1, :], mrow[0:1, 0:512],
                              start=True, stop=True, skip_group_check=True)
                    te.matmul(P2[:, 512:1024], ones1[0:1, :], mrow[0:1, 512:1024],
                              start=True, stop=True, skip_group_check=True)
                    te.matmul(P3[:, 0:512], ones1[0:1, :], trow[0:1, 0:512],
                              start=True, stop=True, skip_group_check=True)
                    te.matmul(P3[:, 512:1024], ones1[0:1, :], trow[0:1, 512:1024],
                              start=True, stop=True,
                              skip_group_check=True).then_inc(PE.h, 1)
                    PE.inc()
                bc_pe = PE.n
                d3_ = DVE.n
                y2d = {}

                if to_y2t:
                    @block.scalar
                    def _(sc):
                        for dc in range(DC):
                            sc.wait_ge(DVE.h, d3_ + dc + 1)
                            sc.dma_start(out=y2t[dc, :, tok0:tok0 + 1024],
                                         in_=y2st[dc % 2][:, :]).then_inc(KQ.h, 16)
                            KQ.inc(16)
                            y2d[dc] = KQ.n

                @block.vector
                def _(ve):
                    ve.wait_ge(PE.h, bc_pe)
                    for dc in range(DC):
                        tn = scr[2]
                        ve.tensor_tensor(tn[:, :],
                                         R[:, dc * 1024:(dc + 1) * 1024],
                                         P2[:, :], op=ALU.subtract)
                        if to_y2t:
                            if dc >= 2:
                                ve.wait_ge(KQ.h, y2d[dc - 2])
                            ve.tensor_tensor(y2st[dc % 2][:, :], tn[:, :],
                                             P3[:, :],
                                             op=ALU.mult).then_inc(DVE.h, 1)
                        else:
                            ve.tensor_tensor(yb[:, dc * 1024:(dc + 1) * 1024],
                                             tn[:, :], P3[:, :],
                                             op=ALU.mult).then_inc(DVE.h, 1)
                        DVE.inc()

                if spill_r and to_y2t:
                    @block.sync
                    def _(sy):
                        for dc in range(DC):
                            sy.dma_start(out=r_sp[dc, :, :],
                                         in_=R[:, dc * 1024:(dc + 1) * 1024]
                                         ).then_inc(SY.h, 16)
                            SY.inc(16)

        emit_ln(False)
        barrier()

        if DEBUG_DUMP == "yb2" and final_out:
            with nc.Block() as block:
                @block.gpsimd
                def _(gp):
                    for dc in range(DC):
                        gp.dma_start(out=out_d[dc, :, :],
                                     in_=yb[:, dc * 1024:(dc + 1) * 1024]
                                     ).then_inc(W.h, 16)
                        W.inc(16)
                    gp.wait_ge(W.h, W.n)
            barrier()
            stk.close()
            return

        if stage == 0:
            s0 = SY.n
            with nc.Block() as block:
                @block.gpsimd
                def _(gp):
                    for dc in range(DC):
                        gp.dma_start(out=out_d[dc, :, :],
                                     in_=yb[:, dc * 1024:(dc + 1) * 1024]
                                     ).then_inc(W.h, 16)
                        W.inc(16)
                    gp.wait_ge(W.h, W.n)
            barrier()
            stk.close()
            return

        with nc.Block() as block:
            pe0, act0, dve0 = PE.n, ACT.n, DVE.n

            @block.tensor
            def _(te):
                for s8 in range(8):
                    if s8 >= 1:
                        te.wait_ge(ACT.h, act0 + s8)
                    for dc in range(DC):
                        mm = te.matmul(
                            P0[:, 0:8],
                            yb[:, dc * 1024 + s8 * 128:dc * 1024 + (s8 + 1) * 128],
                            gate_T[:, dc * 8:(dc + 1) * 8],
                            start=(dc == 0), stop=(dc == DC - 1),
                            skip_group_check=True)
                    mm.then_inc(PE.h, 1)
                    PE.inc()

            @block.scalar
            def _(sc):
                for s8 in range(8):
                    sc.wait_ge(PE.h, pe0 + s8 + 1)
                    sc.activation(g_tok[:, s8 * 8:(s8 + 1) * 8], P0[:, 0:8],
                                  AF.Sigmoid).then_inc(ACT.h, 1)
                    ACT.inc()

            @block.vector
            def _(ve):
                ME = sem["me"]

                def edge(ins):
                    ins.then_inc(ME.h, 1)
                    ve.wait_ge(ME.h, ME.inc())

                for s8 in range(8):
                    ve.wait_ge(ACT.h, act0 + s8 + 1)
                    gg = g_tok[:, s8 * 8:(s8 + 1) * 8]
                    edge(ve.reduce_max(gm1[:, :], gg, axis=AX.X))
                    edge(ve.tensor_scalar(gtmp[:, :], gg, gm1[:, 0:1], None,
                                          op0=ALU.is_ge))
                    edge(ve.scalar_tensor_tensor(gtmp[:, :], gtmp[:, :], -1e30,
                                                 gg, op0=ALU.mult, op1=ALU.add))
                    edge(ve.reduce_max(gm2[:, :], gtmp[:, :], axis=AX.X))
                    edge(ve.tensor_scalar(gtmp[:, :], gg, gm2[:, 0:1], None,
                                          op0=ALU.is_ge))
                    ve.tensor_tensor(gg, gg, gtmp[:, :],
                                     op=ALU.mult).then_inc(DVE.h, 1)
                    DVE.inc()

            pe1, act1 = PE.n, ACT.n

            @block.tensor
            def _(te):
                for s8 in range(8):
                    te.wait_ge(DVE.h, dve0 + s8 + 1)
                    if s8 >= 1:
                        te.wait_ge(ACT.h, act1 + s8)
                    te.transpose(P1[0:8, 0:128], g_tok[:, s8 * 8:(s8 + 1) * 8],
                                 idf[:, :]).then_inc(PE.h, 1)
                    PE.inc()

            @block.scalar
            def _(sc):
                for s8 in range(8):
                    sc.wait_ge(PE.h, pe1 + s8 + 1)
                    sc.activation(g_em[:, s8 * 128:(s8 + 1) * 128],
                                  P1[0:8, 0:128], AF.Copy).then_inc(ACT.h, 1)
                    ACT.inc()
        barrier()

        if DEBUG_DUMP == "g2" and final_out:
            with nc.Block() as block:
                @block.gpsimd
                def _(gp):
                    gp.dma_start(out=out_d[0, :, 0:64],
                                 in_=g_tok[:, :]).then_inc(W.h, 16)
                    W.inc(16)
                    gp.dma_start(out=out_d[1, 0:8, :],
                                 in_=g_em[:, :]).then_inc(W.h, 16)
                    W.inc(16)
                    gp.wait_ge(W.h, W.n)
            barrier()
            stk.close()
            return

        if stage == -1:
            with nc.Block() as block:
                @block.gpsimd
                def _(gp):
                    gp.dma_start(out=out_d[0, :, 0:64],
                                 in_=g_tok[:, :]).then_inc(W.h, 16)
                    W.inc(16)
                    gp.dma_start(out=out_d[1, 0:8, :],
                                 in_=g_em[:, :]).then_inc(W.h, 16)
                    W.inc(16)
                    gp.wait_ge(W.h, W.n)
            barrier()
            stk.close()
            return

        b_zf1, b_zf3, b_wf = sem["zf1"].n, sem["zf3"].n, sem["wf"].n
        b_zd, b_zpf = sem["zd"].n, sem["zpf"].n
        b_a1, b_ss, b_hs = sem["a1"].n, sem["ss"].n, sem["hs"].n
        b_pod, b_pof = sem["pod"].n, sem["pof"].n
        b_gbc, b_gbm = sem["gbc"].n, sem["gbm"].n
        w_load = {}

        with nc.Block() as block:
            n_gb = [0]

            @block.gpsimd
            def _(gp):
                for li, e in enumerate(EXP_LIST):
                    if e < NE:
                        gp.wait_ge(sem["gbm"].h, b_gbm + n_gb[0])
                        n_gb[0] += 1
                        gp.dma_start(out=g_row[:, :],
                                     in_=g_em[e:e + 1, :]).then_inc(W.h, 16)
                        W.inc(16)
                        w_load[("g", e)] = W.n
                    for p4 in range(4):
                        for kind in (0, 1):
                            seq = li * 4 + p4
                            if seq >= 2:
                                gp.wait_ge(sem["zd"].h, b_zd + 2 * seq - 2)
                            if e < NE:
                                src = (w1s if kind == 0 else w3s)[
                                    e * 8 + 2 * p4:e * 8 + 2 * p4 + 2, :, :]
                            else:
                                src = (sw1s if kind == 0 else sw3s)[
                                    2 * p4:2 * p4 + 2, :, :]
                            gp.dma_start(
                                out=(w1b if kind == 0 else w3b)[p4 % 2][:, :]
                                .rearrange("p (h f) -> p h f", h=2),
                                in_=src.rearrange("h p f -> p h f")
                            ).then_inc(W.h, 16)
                            W.inc(16)
                            w_load[(kind, e, p4)] = W.n
                    for d8 in range(8):
                        seq = li * 8 + d8
                        if seq >= 2:
                            gp.wait_ge(sem["pod"].h, b_pod + 2 * seq - 2)
                        if e < NE:
                            src = w2s[e * 16 + 2 * d8:e * 16 + 2 * d8 + 2, :, :]
                        else:
                            src = sw2s[2 * d8:2 * d8 + 2, :, :]
                        gp.dma_start(out=w2b[d8 % 2][:, :]
                                     .rearrange("p (d f) -> p d f", d=2),
                                     in_=src.rearrange("d p f -> p d f")
                                     ).then_inc(W.h, 16)
                        W.inc(16)
                        w_load[(2, e, d8)] = W.n

            n_gbt = [0]

            @block.tensor
            def _(te):
                for li, e in enumerate(EXP_LIST):
                    if li >= 1:
                        te.wait_ge(sem["hs"].h, b_hs + li * HC)
                        te.wait_ge(sem["pof"].h, b_pof + li * DC)
                    if e < NE:
                        te.wait_ge(W.h, w_load[("g", e)])
                        te.matmul(P2[:, 0:512], ones1[0:1, :], g_row[0:1, 0:512],
                                  start=True, stop=True, skip_group_check=True)
                        te.matmul(P2[:, 512:1024], ones1[0:1, :],
                                  g_row[0:1, 512:1024], start=True, stop=True,
                                  skip_group_check=True).then_inc(sem["gbm"].h, 1)
                        sem["gbm"].inc()
                    for hc in range(HC):
                        ghc = li * HC + hc
                        pz1 = P0 if hc % 2 == 0 else P1
                        pz3 = P2 if hc % 2 == 0 else P3
                        if ghc >= 2:
                            te.wait_ge(sem["hs"].h, b_hs + ghc - 1)
                        te.wait_ge(W.h, w_load[(0, e, hc // 2)])
                        wt = w1b[(hc // 2) % 2]
                        off = (hc % 2) * 2048
                        for dc in range(DC):
                            lt = wt[:, off + dc * 128:off + (dc + 1) * 128]
                            st, sp = dc == 0, dc == DC - 1
                            te.matmul(pz1[:, 0:512], lt,
                                      yb[:, dc * 1024:dc * 1024 + 512],
                                      start=st, stop=sp, skip_group_check=True)
                            te.matmul(pz1[:, 512:1024], lt,
                                      yb[:, dc * 1024 + 512:(dc + 1) * 1024],
                                      start=st, stop=sp, skip_group_check=True)
                        if e < NE and hc == 0:
                            te.wait_ge(sem["gbc"].h, b_gbc + n_gbt[0] + 1)
                        te.wait_ge(W.h, w_load[(1, e, hc // 2)])
                        wt = w3b[(hc // 2) % 2]
                        for dc in range(DC):
                            lt = wt[:, off + dc * 128:off + (dc + 1) * 128]
                            st, sp = dc == 0, dc == DC - 1
                            te.matmul(pz3[:, 0:512], lt,
                                      yb[:, dc * 1024:dc * 1024 + 512],
                                      start=st, stop=sp, skip_group_check=True)
                            mm = te.matmul(pz3[:, 512:1024], lt,
                                           yb[:, dc * 1024 + 512:(dc + 1) * 1024],
                                           start=st, stop=sp,
                                           skip_group_check=True)
                        mm.then_inc(sem["zd"].h, 1)
                        sem["zd"].inc()
                    if e < NE:
                        n_gbt[0] += 1
                    for dc in range(DC):
                        gdc = li * DC + dc
                        po = P0 if dc % 2 == 0 else P1
                        if gdc >= 2:
                            te.wait_ge(sem["pof"].h, b_pof + gdc - 1)
                        if dc == 0:
                            te.wait_ge(sem["hs"].h, b_hs + li * HC + 7)
                        elif dc == 1:
                            te.wait_ge(sem["hs"].h, b_hs + li * HC + 8)
                        te.wait_ge(W.h, w_load[(2, e, dc // 2)])
                        wt = w2b[(dc // 2) % 2]
                        off = (dc % 2) * 1024
                        for hc in range(HC):
                            te.wait_ge(sem["hs"].h, b_hs + li * HC + hc + 1)
                            lt = wt[:, off + hc * 128:off + (hc + 1) * 128]
                            st, sp = hc == 0, hc == HC - 1
                            te.matmul(po[:, 0:512], lt,
                                      he[:, hc * 1024:hc * 1024 + 512],
                                      start=st, stop=sp, skip_group_check=True)
                            mm = te.matmul(po[:, 512:1024], lt,
                                           he[:, hc * 1024 + 512:(hc + 1) * 1024],
                                           start=st, stop=sp,
                                           skip_group_check=True)
                        mm.then_inc(sem["pod"].h, 1)
                        sem["pod"].inc()

            n_gbs = [0]

            @block.scalar
            def _(sc):
                for li, e in enumerate(EXP_LIST):
                    if e < NE:
                        sc.wait_ge(sem["gbm"].h, b_gbm + n_gbs[0] + 1)
                        n_gbs[0] += 1
                        if li >= 2:
                            sc.wait_ge(sem["hs"].h, b_hs + (li - 1) * HC)
                        sc.activation(gb_sb[li % 2][:, :], P2[:, :],
                                      AF.Copy).then_inc(sem["gbc"].h, 1)
                        sem["gbc"].inc()
                    for hc in range(HC):
                        ghc = li * HC + hc
                        if ghc >= 1:
                            sc.wait_ge(sem["hs"].h, b_hs + ghc)
                        if e < NE:
                            sc.wait_ge(sem["a1"].h, b_a1 + ghc + 1)
                            sc.activation(scr[1][:, :], scr[0][:, :],
                                          AF.Silu).then_inc(sem["ss"].h, 1)
                        else:
                            sc.wait_ge(sem["zd"].h, b_zd + ghc + 1)
                            pz1 = P0 if hc % 2 == 0 else P1
                            sc.activation(scr[1][:, :], pz1[:, :],
                                          AF.Silu).then_inc(sem["ss"].h, 1)
                        sem["ss"].inc()

            n_gbv = [0]

            @block.vector
            def _(ve):
                for li, e in enumerate(EXP_LIST):
                    if e < NE:
                        n_gbv[0] += 1
                    for hc in range(HC):
                        ghc = li * HC + hc
                        pz1 = P0 if hc % 2 == 0 else P1
                        pz3 = P2 if hc % 2 == 0 else P3
                        ve.wait_ge(sem["zd"].h, b_zd + ghc + 1)
                        if e < NE:
                            ve.wait_ge(sem["gbc"].h, b_gbc + n_gbv[0])
                            if ghc >= 1:
                                ve.wait_ge(sem["ss"].h, b_ss + ghc)
                            ve.tensor_tensor(scr[0][:, :], pz1[:, :],
                                             gb_sb[li % 2][:, :],
                                             op=ALU.mult).then_inc(sem["a1"].h, 1)
                            sem["a1"].inc()
                            ve.tensor_tensor(scr[2][:, :], pz3[:, :],
                                             gb_sb[li % 2][:, :], op=ALU.mult)
                            ve.wait_ge(sem["ss"].h, b_ss + ghc + 1)
                            ve.scalar_tensor_tensor(
                                he[:, hc * 1024:(hc + 1) * 1024], scr[2][:, :],
                                0.5, scr[1][:, :], op0=ALU.mult,
                                op1=ALU.mult).then_inc(sem["hs"].h, 1)
                        else:
                            ve.wait_ge(sem["ss"].h, b_ss + ghc + 1)
                            ve.scalar_tensor_tensor(
                                he[:, hc * 1024:(hc + 1) * 1024], pz3[:, :],
                                0.5, scr[1][:, :], op0=ALU.mult,
                                op1=ALU.mult).then_inc(sem["hs"].h, 1)
                        sem["hs"].inc()
                    for dc in range(DC):
                        gdc = li * DC + dc
                        po = P0 if dc % 2 == 0 else P1
                        ve.wait_ge(sem["pod"].h, b_pod + gdc + 1)
                        ve.tensor_tensor(R[:, dc * 1024:(dc + 1) * 1024],
                                         po[:, :],
                                         R[:, dc * 1024:(dc + 1) * 1024],
                                         op=ALU.add).then_inc(sem["pof"].h, 1)
                        sem["pof"].inc()
        barrier()

        if DEBUG_DUMP == "he" or (DEBUG_DUMP == "he2" and final_out):
            with nc.Block() as block:
                @block.gpsimd
                def _(gp):
                    for hc in range(HC):
                        gp.dma_start(out=out_d[hc, :, :],
                                     in_=he[:, hc * 1024:(hc + 1) * 1024]
                                     ).then_inc(W.h, 16)
                        W.inc(16)
                    gp.wait_ge(W.h, W.n)
            barrier()
            stk.close()
            return

        if produce_y2:
            emit_ln(True)
            barrier()
        if final_out:
            s0 = SY.n
            nexp = len(EXP_LIST)
            b_pof2 = sem["pof"].n - nexp * DC
            with nc.Block() as block:
                @block.sync
                def _(sy):
                    for dc in range(DC):
                        sy.wait_ge(sem["pof"].h, b_pof2 + (nexp - 1) * DC + dc + 1)
                        sy.dma_start(out=out_d[dc, :, :],
                                     in_=R[:, dc * 1024:(dc + 1) * 1024]
                                     ).then_inc(SY.h, 16)
                        SY.inc(16)
                    sy.wait_ge(SY.h, s0 + 16 * DC)
            barrier()
        stk.close()

    def emit_attention(R2):
        astk = ExitStack()

        def asb(n, s_, d):
            return astk.enter_context(nc.sbuf_tensor("at_" + n, s_, d))

        attn_fm = asb("fm", [128, 16 * 1024], BF16)
        v_res = asb("vres", [128, 16 * 1024], BF16)

        with ExitStack() as vstk:
            y2o = vstk.enter_context(nc.sbuf_tensor("a1y", [128, DC * 1024], BF16))
            wvb = [vstk.enter_context(nc.sbuf_tensor(f"a1w{i}", [128, 2048], BF16))
                   for i in range(2)]
            vst = [vstk.enter_context(nc.sbuf_tensor(f"a1v{i}", [128, 1024], BF16))
                   for i in range(2)]
            PV = [vstk.enter_context(nc.psum_tensor(f"a1p{i}", [128, 1024], F32))
                  for i in range(2)]
            with nc.Block() as block:
                pe0, act0 = PE.n, ACT.n
                st_cnt = {}
                wv_cnt = {}
                y2o_cnt = None

                @block.sync
                def _(sy):
                    nonlocal y2o_cnt
                    sy.dma_start(out=y2o[:, :].rearrange(
                                     "p (d s) -> p d s", d=16),
                                 in_=y2t[:, :, 0:1024].rearrange(
                                     "d p s -> p d s")).then_inc(SY.h, 16)
                    SY.inc(16)
                    y2o_cnt = SY.n
                    for fc in range(16):
                        sy.wait_ge(ACT.h, act0 + fc + 1)
                        sy.dma_start(out=v_sp[fc, :, :],
                                     in_=vst[fc % 2][:, :]).then_inc(SY.h, 16)
                        SY.inc(16)
                        st_cnt[fc] = SY.n

                @block.gpsimd
                def _(gp):
                    for fc in range(16):
                        if fc >= 2:
                            gp.wait_ge(PE.h, pe0 + fc - 1)
                        gp.dma_start(out=wvb[fc % 2][:, :],
                                     in_=wvs[fc, :, :]).then_inc(W.h, 16)
                        W.inc(16)
                        wv_cnt[fc] = W.n

                @block.tensor
                def _(te):
                    te.wait_ge(SY.h, y2o_cnt)
                    for fc in range(16):
                        te.wait_ge(W.h, wv_cnt[fc])
                        if fc >= 2:
                            te.wait_ge(ACT.h, act0 + fc - 1)
                        pv = PV[fc % 2]
                        for dc in range(DC):
                            lt = wvb[fc % 2][:, dc * 128:(dc + 1) * 128]
                            st, sp = dc == 0, dc == DC - 1
                            te.matmul(pv[:, 0:512], lt,
                                      y2o[:, dc * 1024:dc * 1024 + 512],
                                      start=st, stop=sp, skip_group_check=True)
                            mm = te.matmul(pv[:, 512:1024], lt,
                                           y2o[:, dc * 1024 + 512:(dc + 1) * 1024],
                                           start=st, stop=sp,
                                           skip_group_check=True)
                        mm.then_inc(PE.h, 1)
                        PE.inc()

                @block.scalar
                def _(sc):
                    for fc in range(16):
                        sc.wait_ge(PE.h, pe0 + fc + 1)
                        if fc >= 2:
                            sc.wait_ge(SY.h, st_cnt[fc - 2])
                        sc.activation(vst[fc % 2][:, :], PV[fc % 2][:, :],
                                      AF.Copy).then_inc(ACT.h, 1)
                        ACT.inc()
            barrier()

        for which, wsrc, dst, qs in [("q", wq_d, q_dr, 0.125),
                                     ("k", wk_d, k_dr, 1.0)]:
            for eh in range(2):
                with ExitStack() as qstk:
                    wres = qstk.enter_context(nc.sbuf_tensor(
                        f"a2w_{which}{eh}", [128, DC * 1024], BF16))
                    ysl = [qstk.enter_context(nc.sbuf_tensor(
                        f"a2y{i}_{which}{eh}", [128, 2048], BF16))
                        for i in range(2)]
                    qst = [qstk.enter_context(nc.sbuf_tensor(
                        f"a2s{i}_{which}{eh}", [128, 1024], BF16))
                        for i in range(2)]
                    PQ = [qstk.enter_context(nc.psum_tensor(
                        f"a2p{i}_{which}{eh}", [128, 1024], F32))
                        for i in range(2)]
                    with nc.Block() as block:
                        pe0, act0 = PE.n, ACT.n
                        w0 = W.n
                        ld_cnt = {}
                        st_cnt = {}

                        @block.gpsimd
                        def _(gp):
                            for q4 in range(4):
                                gp.dma_start(
                                    out=wres[:, q4 * 4096:(q4 + 1) * 4096]
                                    .rearrange("p (dc e) -> p dc e", dc=4),
                                    in_=wsrc[q4 * 512:(q4 + 1) * 512,
                                             eh * 1024:(eh + 1) * 1024]
                                    .rearrange("(dc p) e -> p dc e", p=128)
                                ).then_inc(W.h, 16)
                                W.inc(16)

                        @block.sync
                        def _(sy):
                            for s16 in range(16):
                                if s16 >= 2:
                                    sy.wait_ge(PE.h, pe0 + s16 - 1)
                                sy.dma_start(
                                    out=ysl[s16 % 2][:, :].rearrange(
                                        "p (d s) -> p d s", d=16),
                                    in_=y2t[:, :, s16 * 128:(s16 + 1) * 128]
                                    .rearrange("d p s -> p d s")
                                ).then_inc(SY.h, 16)
                                SY.inc(16)
                                ld_cnt[s16] = SY.n
                                if s16 >= 1:
                                    j = s16 - 1
                                    sy.wait_ge(ACT.h, act0 + j + 1)
                                    sy.dma_start(
                                        out=dst[j, :, eh * 1024:(eh + 1) * 1024],
                                        in_=qst[j % 2][:, :]).then_inc(SY.h, 16)
                                    SY.inc(16)
                                    st_cnt[j] = SY.n
                            sy.wait_ge(ACT.h, act0 + 16)
                            sy.dma_start(out=dst[15, :, eh * 1024:(eh + 1) * 1024],
                                         in_=qst[1][:, :]).then_inc(SY.h, 16)
                            SY.inc(16)
                            st_cnt[15] = SY.n

                        @block.tensor
                        def _(te):
                            te.wait_ge(W.h, w0 + 64)
                            for s16 in range(16):
                                te.wait_ge(SY.h, ld_cnt[s16])
                                if s16 >= 2:
                                    te.wait_ge(ACT.h, act0 + s16 - 1)
                                pq = PQ[s16 % 2]
                                for dc in range(DC):
                                    lt = ysl[s16 % 2][:, dc * 128:(dc + 1) * 128]
                                    st, sp = dc == 0, dc == DC - 1
                                    te.matmul(pq[:, 0:512], lt,
                                              wres[:, dc * 1024:dc * 1024 + 512],
                                              start=st, stop=sp,
                                              skip_group_check=True)
                                    mm = te.matmul(
                                        pq[:, 512:1024], lt,
                                        wres[:, dc * 1024 + 512:(dc + 1) * 1024],
                                        start=st, stop=sp, skip_group_check=True)
                                mm.then_inc(PE.h, 1)
                                PE.inc()

                        @block.scalar
                        def _(sc):
                            for s16 in range(16):
                                sc.wait_ge(PE.h, pe0 + s16 + 1)
                                if s16 >= 2:
                                    sc.wait_ge(SY.h, st_cnt[s16 - 2])
                                if qs != 1.0:
                                    sc.activation(qst[s16 % 2][:, :],
                                                  PQ[s16 % 2][:, :], AF.Copy,
                                                  scale=qs,
                                                  bias=0.0).then_inc(ACT.h, 1)
                                else:
                                    sc.activation(qst[s16 % 2][:, :],
                                                  PQ[s16 % 2][:, :],
                                                  AF.Copy).then_inc(ACT.h, 1)
                                ACT.inc()
                    barrier()

        with ExitStack() as sstk:
            eth = sstk.enter_context(nc.sbuf_tensor("a4e", [128, 16 * 1024], BF16))
            qh = sstk.enter_context(nc.sbuf_tensor("a4q", [128, 16 * 1024], BF16))
            kcb = [sstk.enter_context(nc.sbuf_tensor(f"a4k{i}", [128, 2048], BF16))
                   for i in range(2)]
            PS = [sstk.enter_context(nc.psum_tensor(f"a4p{i}", [128, 1024], F32))
                  for i in range(2)]
            PZT = sstk.enter_context(nc.psum_tensor("a4z", [128, 16], F32))
            vres_cnt = None
            with nc.Block() as block:
                @block.sync
                def _(sy):
                    nonlocal vres_cnt
                    sy.dma_start(out=v_res[:, :].rearrange(
                                     "p (f s) -> p f s", f=16),
                                 in_=v_sp.rearrange("f p s -> p f s")
                                 ).then_inc(SY.h, 16)
                    SY.inc(16)
                    vres_cnt = SY.n

            a6_done_pe = None
            for eh in range(2):
                with nc.Block() as block:
                    pe0, act0 = PE.n, ACT.n
                    qh_cnt = None
                    kc_cnt = {}
                    prev_a6 = a6_done_pe

                    @block.sync
                    def _(sy):
                        nonlocal qh_cnt
                        sy.dma_start(out=qh[:, :].rearrange(
                                         "p (s e) -> p s e", s=16),
                                     in_=q_dr[:, :, eh * 1024:(eh + 1) * 1024]
                                     .rearrange("s p e -> p s e")
                                     ).then_inc(SY.h, 16)
                        SY.inc(16)
                        qh_cnt = SY.n
                        for fc in range(16):
                            if fc >= 2:
                                sy.wait_ge(PE.h, pe0 + fc - 1)
                            sy.dma_start(
                                out=kcb[fc % 2][:, :].rearrange(
                                    "p (s f) -> p s f", s=16),
                                in_=k_dr[:, :, fc * 128:(fc + 1) * 128]
                                .rearrange("s p f -> p s f")).then_inc(SY.h, 16)
                            SY.inc(16)
                            kc_cnt[fc] = SY.n

                    @block.tensor
                    def _(te):
                        te.wait_ge(SY.h, qh_cnt)
                        for fc in range(16):
                            te.wait_ge(SY.h, kc_cnt[fc])
                            if fc >= 2:
                                te.wait_ge(ACT.h, act0 + fc - 1)
                            pss = PS[fc % 2]
                            for s16 in range(16):
                                lt = kcb[fc % 2][:, s16 * 128:(s16 + 1) * 128]
                                st, sp = s16 == 0, s16 == 15
                                te.matmul(pss[:, 0:512], lt,
                                          qh[:, s16 * 1024:s16 * 1024 + 512],
                                          start=st, stop=sp,
                                          skip_group_check=True)
                                mm = te.matmul(
                                    pss[:, 512:1024], lt,
                                    qh[:, s16 * 1024 + 512:(s16 + 1) * 1024],
                                    start=st, stop=sp, skip_group_check=True)
                            mm.then_inc(PE.h, 1)
                            PE.inc()
                            if fc >= 1:
                                te.wait_ge(ACT.h, act0 + fc)
                                for e8 in range(8):
                                    te.matmul(
                                        PZT[:, eh * 8 + e8:eh * 8 + e8 + 1],
                                        eth[:, (fc - 1) * 1024 + e8 * 128:
                                            (fc - 1) * 1024 + (e8 + 1) * 128],
                                        onescol_bf[:, 0:1],
                                        start=(eh == 0 and fc == 1 and e8 == 0),
                                        stop=False, skip_group_check=True)
                        te.wait_ge(ACT.h, act0 + 16)
                        for e8 in range(8):
                            mm = te.matmul(
                                PZT[:, eh * 8 + e8:eh * 8 + e8 + 1],
                                eth[:, 15 * 1024 + e8 * 128:
                                    15 * 1024 + (e8 + 1) * 128],
                                onescol_bf[:, 0:1], start=False, stop=True,
                                skip_group_check=True)
                        mm.then_inc(PE.h, 1)
                        PE.inc()

                    z_pe = PE.n

                    @block.scalar
                    def _(sc):
                        for fc in range(16):
                            sc.wait_ge(PE.h, pe0 + fc + 1)
                            if prev_a6 is not None:
                                sc.wait_ge(PE.h, prev_a6)
                            sc.activation(eth[:, fc * 1024:(fc + 1) * 1024],
                                          PS[fc % 2][:, :], AF.Exp,
                                          bias=expbias[:, 0:1]).then_inc(ACT.h, 1)
                            ACT.inc()

                    @block.vector
                    def _(ve):
                        ve.wait_ge(PE.h, z_pe)
                        ve.reciprocal(rz_sb[:, eh * 8:(eh + 1) * 8],
                                      PZT[:, eh * 8:(eh + 1) * 8]
                                      ).then_inc(DVE.h, 1)
                        DVE.inc()

                    rz_dve = DVE.n
                    a6pe0 = PE.n
                    a6act0 = ACT.n

                    @block.tensor
                    def _(te):
                        te.wait_ge(ACT.h, act0 + 16)
                        te.wait_ge(SY.h, vres_cnt)
                        for e8 in range(8):
                            if e8 >= 2:
                                te.wait_ge(ACT.h, a6act0 + e8 - 1)
                            pa = PS[e8 % 2]
                            for fc in range(16):
                                lt = eth[:, fc * 1024 + e8 * 128:
                                         fc * 1024 + (e8 + 1) * 128]
                                st, sp = fc == 0, fc == 15
                                te.matmul(pa[:, 0:512], lt,
                                          v_res[:, fc * 1024:fc * 1024 + 512],
                                          start=st, stop=sp,
                                          skip_group_check=True)
                                mm = te.matmul(
                                    pa[:, 512:1024], lt,
                                    v_res[:, fc * 1024 + 512:(fc + 1) * 1024],
                                    start=st, stop=sp, skip_group_check=True)
                            mm.then_inc(PE.h, 1)
                            PE.inc()

                    a6_done_pe = PE.n

                    @block.scalar
                    def _(sc):
                        sc.wait_ge(DVE.h, rz_dve)
                        for e8 in range(8):
                            ge = eh * 8 + e8
                            sc.wait_ge(PE.h, a6pe0 + e8 + 1)
                            sc.activation(attn_fm[:, ge * 1024:(ge + 1) * 1024],
                                          PS[e8 % 2][:, :], AF.Copy,
                                          scale=rz_sb[:, ge:ge + 1],
                                          bias=0.0).then_inc(ACT.h, 1)
                            ACT.inc()
                barrier()

        with ExitStack() as ostk:
            wob = [ostk.enter_context(nc.sbuf_tensor(f"a7w{i}", [128, 2048], BF16))
                   for i in range(2)]
            PW = [ostk.enter_context(nc.psum_tensor(f"a7p{i}", [128, 1024], F32))
                  for i in range(2)]
            with nc.Block() as block:
                pe0, dve0 = PE.n, DVE.n
                r_cnt = {}
                wo_cnt = {}

                @block.sync
                def _(sy):
                    for gc in range(16):
                        sy.dma_start(out=R2[:, gc * 1024:(gc + 1) * 1024],
                                     in_=r_sp[gc, :, :]).then_inc(SY.h, 16)
                        SY.inc(16)
                        r_cnt[gc] = SY.n

                @block.gpsimd
                def _(gp):
                    for gc in range(16):
                        if gc >= 2:
                            gp.wait_ge(PE.h, pe0 + gc - 1)
                        gp.dma_start(out=wob[gc % 2][:, :],
                                     in_=wos[gc, :, :]).then_inc(W.h, 16)
                        W.inc(16)
                        wo_cnt[gc] = W.n

                @block.tensor
                def _(te):
                    for gc in range(16):
                        te.wait_ge(W.h, wo_cnt[gc])
                        if gc >= 2:
                            te.wait_ge(DVE.h, dve0 + gc - 1)
                        pw = PW[gc % 2]
                        for ec in range(16):
                            lt = wob[gc % 2][:, ec * 128:(ec + 1) * 128]
                            st, sp = ec == 0, ec == 15
                            te.matmul(pw[:, 0:512], lt,
                                      attn_fm[:, ec * 1024:ec * 1024 + 512],
                                      start=st, stop=sp, skip_group_check=True)
                            mm = te.matmul(
                                pw[:, 512:1024], lt,
                                attn_fm[:, ec * 1024 + 512:(ec + 1) * 1024],
                                start=st, stop=sp, skip_group_check=True)
                        mm.then_inc(PE.h, 1)
                        PE.inc()

                @block.vector
                def _(ve):
                    for gc in range(16):
                        ve.wait_ge(PE.h, pe0 + gc + 1)
                        ve.wait_ge(SY.h, r_cnt[gc])
                        ve.tensor_tensor(R2[:, gc * 1024:(gc + 1) * 1024],
                                         PW[gc % 2][:, :],
                                         R2[:, gc * 1024:(gc + 1) * 1024],
                                         op=ALU.add).then_inc(DVE.h, 1)
                        DVE.inc()
            barrier()
        astk.close()

    if stage <= 1:
        with ExitStack() as rstk:
            R = rstk.enter_context(nc.sbuf_tensor("Rr", [128, DC * 1024], F32))
            emit_ffn_pass(0, produce_y2=False, spill_r=False, final_out=True,
                          R=R, load_x=True)
    else:
        with ExitStack() as rstk:
            R = rstk.enter_context(nc.sbuf_tensor("Rr", [128, DC * 1024], F32))
            emit_ffn_pass(1024, produce_y2=True, spill_r=False, final_out=False,
                          R=R, load_x=True)
            emit_ffn_pass(0, produce_y2=True, spill_r=True, final_out=False,
                          R=R, load_x=True)
        with ExitStack() as rstk:
            R2 = rstk.enter_context(nc.sbuf_tensor("R2r", [128, DC * 1024], F32))
            emit_attention(R2)
            if stage == 2:
                s0 = SY.n
                with nc.Block() as block:
                    @block.sync
                    def _(sy):
                        for dc in range(DC):
                            sy.dma_start(out=out_d[dc, :, :],
                                         in_=R2[:, dc * 1024:(dc + 1) * 1024]
                                         ).then_inc(SY.h, 16)
                            SY.inc(16)
                        sy.wait_ge(SY.h, s0 + 16 * DC)
                barrier()
            else:
                emit_ffn_pass(0, produce_y2=False, spill_r=False,
                              final_out=True, R=R2, load_x=False)
    g.close()
    return nc


_PREP_CACHE = {}


def _prep_shared(inputs):
    key = id(inputs.get("L0_attn_Wq"))
    if key in _PREP_CACHE:
        return _PREP_CACHE[key]
    f32 = np.float32
    Wq = np.ascontiguousarray(inputs["L0_attn_Wq"], dtype=f32)
    Wk = np.ascontiguousarray(inputs["L0_attn_Wk"], dtype=f32)
    Wv = np.asarray(inputs["L0_attn_Wv"], dtype=f32)
    Wo = np.asarray(inputs["L0_attn_Wo"], dtype=f32)
    w1 = np.asarray(inputs["L0_ffn_w1"], dtype=f32)
    w2 = np.asarray(inputs["L0_ffn_w2"], dtype=f32)
    w3 = np.asarray(inputs["L0_ffn_w3"], dtype=f32)
    sw1 = np.asarray(inputs["L0_ffn_sw1"], dtype=f32)
    sw2 = np.asarray(inputs["L0_ffn_sw2"], dtype=f32)
    sw3 = np.asarray(inputs["L0_ffn_sw3"], dtype=f32)
    shared = {
        "wq": Wq,
        "wk": Wk,
        "wvs": np.ascontiguousarray(
            Wv.reshape(16, 128, 16, 128).transpose(2, 1, 0, 3).reshape(16, 128, 2048)),
        "wos": np.ascontiguousarray(
            Wo.reshape(16, 128, 16, 128).transpose(2, 1, 0, 3).reshape(16, 128, 2048)),
        "gw": np.ascontiguousarray(inputs["L0_ffn_gate_w"], dtype=f32),
        "w1s": np.ascontiguousarray(
            w1.reshape(8, 8, 128, 16, 128).transpose(0, 1, 4, 3, 2)
            .reshape(64, 128, 2048)),
        "w3s": np.ascontiguousarray(
            w3.reshape(8, 8, 128, 16, 128).transpose(0, 1, 4, 3, 2)
            .reshape(64, 128, 2048)),
        "w2s": np.ascontiguousarray(
            w2.reshape(8, 16, 128, 8, 128).transpose(0, 1, 4, 3, 2)
            .reshape(128, 128, 1024)),
        "sw1s": np.ascontiguousarray(
            sw1.reshape(8, 128, 16, 128).transpose(0, 3, 2, 1).reshape(8, 128, 2048)),
        "sw3s": np.ascontiguousarray(
            sw3.reshape(8, 128, 16, 128).transpose(0, 3, 2, 1).reshape(8, 128, 2048)),
        "sw2s": np.ascontiguousarray(
            sw2.reshape(16, 128, 8, 128).transpose(0, 3, 2, 1).reshape(16, 128, 1024)),
        "ident": np.eye(128, dtype=f32),
    }
    _PREP_CACHE.clear()
    _PREP_CACHE[key] = shared
    return shared


def kernel(**inputs):
    shared = _prep_shared(inputs)
    x = np.asarray(inputs["L0_attn_x"], dtype=np.float32)
    in_maps = []
    for c in range(8):
        b, h = c // 2, c % 2
        xT = x[b].T
        xfm_c = np.ascontiguousarray(np.concatenate(
            [xT[:, h * 1024:(h + 1) * 1024],
             xT[:, (1 - h) * 1024:(2 - h) * 1024]], axis=1))
        m = dict(shared)
        m["xfm"] = xfm_c
        in_maps.append(m)

    nc = build_program(STAGE)
    res = run_bass_kernel_spmd(nc, in_maps, core_ids=list(range(8)))

    y = np.empty((4, 2048, 2048), dtype=np.float32)
    for c in range(8):
        b, h = c // 2, c % 2
        o = np.asarray(res.results[c]["out"]).reshape(2048, 1024)
        y[b, h * 1024:(h + 1) * 1024, :] = o.T
    return y
